# version 15
# baseline (speedup 1.0000x reference)
import numpy as np
import concourse.bass as bass
import concourse.mybir as mybir
from concourse.bass_utils import run_bass_kernel_spmd

F32 = mybir.dt.float32
BF16 = mybir.dt.bfloat16
AF = mybir.ActivationFunctionType
ALU = mybir.AluOpType
AX = mybir.AxisListType

B, L, DM, DI, N, R = 4, 8192, 1024, 2048, 16, 64
DH = DI // 2
NT = DH // 128
LT = L // 2
FI = 3072
MDI, MN, MK = 64, 4, 2
E96 = R + 2 * N
MAGIC = 12582912.0

DEBUG = False

PHASES = {"front": True, "inner": True, "scan": True, "tail": True}
INNER_UPTO = "full"
TAIL_UPTO = "full"


class Sem:
    def __init__(self, h):
        self.h = h
        self.v = 0


class Prog:

    def __init__(self, nc):
        self.nc = nc
        self.streams = {k: [] for k in ("sync", "tensor", "vector", "scalar", "gpsimd")}

    def op(self, eng, f, waits=(), incs=()):
        waits = tuple((s.h, s.v if v is None else v) for (s, v) in waits)
        for s, k in incs:
            s.v += k
        incs = tuple((s.h, k) for (s, k) in incs)

        def run(e, waits=waits, incs=incs, f=f):
            for h, v in waits:
                if v > 0:
                    e.wait_ge(h, v)
            inst = f(e)
            for h, k in incs:
                inst = inst.then_inc(h, k)

        self.streams[eng].append(run)

    def clear(self):
        self.streams = {k: [] for k in self.streams}

    def emit(self):
        if not any(self.streams.values()):
            return
        with self.nc.Block() as block:
            @block.sync
            def _(e):
                for f in self.streams["sync"]:
                    f(e)

            @block.tensor
            def _(e):
                for f in self.streams["tensor"]:
                    f(e)

            @block.vector
            def _(e):
                for f in self.streams["vector"]:
                    f(e)

            @block.scalar
            def _(e):
                for f in self.streams["scalar"]:
                    f(e)

            @block.gpsimd
            def _(e):
                for f in self.streams["gpsimd"]:
                    f(e)


def W(*pairs):
    return tuple((s, None) for s in pairs)


def build_nc():
    nc = bass.Bass("TRN2", target_bir_lowering=False)
    d = {}

    def din(name, shape, dtype=F32):
        d[name] = nc.dram_tensor(name, list(shape), dtype, kind="ExternalInput").ap()

    def dout(name, shape, dtype=F32):
        d[name] = nc.dram_tensor(name, list(shape), dtype, kind="ExternalOutput").ap()

    def dint(name, shape, dtype=F32):
        d[name] = nc.dram_tensor(name, list(shape), dtype, kind="Internal").ap()

    din("xT", (DM, L))
    din("x_nat", (LT, DM))
    din("w_in_T", (DM, 2 * DH))
    din("conv_w", (128, NT * 4))
    din("conv_b", (128, NT))
    din("xproj_T", (DH, E96))
    din("A_logc", (128, NT * N))
    din("Dc", (128, NT))
    din("dtbc", (128, NT))
    din("dtoutw_T", (R, DH))
    din("outw_T", (DI, DM))
    din("m_in_wT", (R, 2 * MDI))
    din("m_conv_w", (MDI, MK))
    din("m_conv_b", (MDI, 1))
    din("m_xp_T", (MDI, 12))
    din("m_dt_wT", (MN, MDI))
    din("m_dt_b", (MDI, 1))
    din("mA", (MDI, MN))
    din("m_D", (MDI, 1))
    din("m_outwT", (MDI, R))
    din("hoff", (1, 1), mybir.dt.int32)
    din("wg_s", (FI, 128))
    din("wu_s", (FI, 128))
    din("wd_s", (DM, FI // 8))

    dout("out", (LT, DM))
    if DEBUG:
        dout("dbg_u", (DH, 512))
        dout("dbg_z", (DH, 512))
        dout("dbg_xdbl", (E96, L))
        dout("dbg_dtm", (R, L), BF16)
        dout("dbg_y", (DH, 1024))
        dout("dbg_ypost", (DH, L), BF16)
        dout("dbg_x1", (LT, DM))

    dint("u_sp", (DH, L))
    dint("z_sp", (DH, L))
    dint("xdbl_i", (E96, L))
    dint("xdbl_o", (E96, L))
    dint("yp_i", (DH, L), BF16)
    dint("yp_o", (DI, L), BF16)
    dint("yp_my", (DI, LT), BF16)
    dint("wgT_i", (128, FI), BF16)
    dint("wgT_o", (DM, FI), BF16)
    dint("wuT_i", (128, FI), BF16)
    dint("wuT_o", (DM, FI), BF16)
    dint("wdT_i", (FI // 8, DM), BF16)
    dint("wdT_o", (FI, DM), BF16)
    dint("gate_sp", (LT, FI))
    dint("up_sp", (LT, FI))
    dint("t_sp", (LT, FI))
    dint("x1_sp", (LT, DM))
    dint("mx1_i", (1, 1)); dint("mx1_o", (1, 1))
    dint("mx2_i", (1, 2)); dint("mx2_o", (1, 2))
    dint("mx3_i", (1, 1)); dint("mx3_o", (1, 1))
    dint("mx4_i", (1, 1)); dint("mx4_o", (1, 1))

    build_program(nc, d)
    return nc


def build_program(nc, d):
    P = Prog(nc)
    op = P.op

    class Scope:
        def __init__(self):
            self.ctx = []

        def sb(self, name, shape, dtype=F32):
            t = nc.sbuf_tensor(name, list(shape), dtype)
            self.ctx.append(t)
            return t.__enter__()

        def psum(self, name, shape, dtype=F32):
            t = nc.psum_tensor(name, list(shape), dtype)
            self.ctx.append(t)
            return t.__enter__()

        def close(self):
            for t in reversed(self.ctx):
                t.__exit__(None, None, None)
            self.ctx = []

    g = Scope()

    def mksem(name):
        t = nc.semaphore(name)
        g.ctx.append(t)
        return Sem(t.__enter__())

    S = dict(
        ld=mksem("ld"), st=mksem("st"), ld2=mksem("ld2"),
        td=mksem("td"), vd=mksem("vd"), sd=mksem("sd"), gd=mksem("gd"),
    )

    R_ = {}
    R_["Ac"] = g.sb("sb_Ac", (128, NT * N))
    R_["Dc"] = g.sb("sb_Dc", (128, NT))
    R_["dtbc"] = g.sb("sb_dtbc", (128, NT))
    R_["dtwT"] = g.sb("sb_dtwT", (R, DH), BF16)
    R_["dtm"] = g.sb("sb_dtm", (R, L), BF16)
    R_["ones"] = g.sb("sb_ones", (128, 128))
    R_["id"] = g.sb("sb_id", (128, 128))
    R_["idb"] = g.sb("sb_idb", (128, 128), BF16)

    ph0_prep(nc, d, P, S, R_, g)

    if PHASES["front"]:
        sc = Scope()
        ph1_front(nc, d, P, S, R_, sc)
        P.emit(); P.clear()
        sc.close()

    if PHASES["inner"]:
        sc = Scope()
        ph2_inner(nc, d, P, S, R_, sc)
        P.emit(); P.clear()
        sc.close()

    if PHASES["scan"]:
        sc = Scope()
        ph3_scan(nc, d, P, S, R_, sc)
        P.emit(); P.clear()
        sc.close()

    if PHASES["tail"]:
        sc = Scope()
        ph4_tail(nc, d, P, S, R_, sc)
        P.emit(); P.clear()
        sc.close()

    op = P.op
    op("sync", lambda e: e.nop(), waits=W(S["st"], S["ld"]))
    op("gpsimd", lambda e: e.nop(), waits=W(S["ld2"], S["gd"]))
    P.emit(); P.clear()
    g.close()


def ph0_prep(nc, d, P, S, R_, g):
    op = P.op
    ld, st, td, vd, sd, gd = S["ld"], S["st"], S["td"], S["vd"], S["sd"], S["gd"]
    s_Ac, s_Dc, s_dtbc = R_["Ac"], R_["Dc"], R_["dtbc"]
    op("sync", lambda e: e.dma_start(out=s_Dc[:], in_=d["Dc"]), incs=[(ld, 16)])
    op("sync", lambda e: e.dma_start(out=s_dtbc[:], in_=d["dtbc"]), incs=[(ld, 16)])
    op("sync", lambda e: e.dma_start(out=s_Ac[:], in_=d["A_logc"]), incs=[(ld, 16)])
    op("scalar", lambda e: e.activation(out=s_Ac[:], in_=s_Ac[:], func=AF.Exp),
       waits=W(ld), incs=[(sd, 1)])
    op("vector", lambda e: e.tensor_scalar(out=s_Ac[:], in0=s_Ac[:], scalar1=-1.0,
                                           scalar2=None, op0=ALU.mult),
       waits=W(sd), incs=[(vd, 1)])
    op("vector", lambda e: e.memset(R_["ones"][:], 1.0), incs=[(vd, 1)])
    op("vector", lambda e: e.memset(R_["id"][:], 1.0), incs=[(vd, 1)])
    op("gpsimd", lambda e: e.affine_select(out=R_["id"][:], in_=R_["id"][:],
                                           pattern=[[-1, 128]], base=0,
                                           channel_multiplier=1,
                                           compare_op=ALU.is_equal, fill=0.0),
       waits=W(vd), incs=[(gd, 1)])
    op("scalar", lambda e: e.copy(out=R_["idb"][:], in_=R_["id"][:]),
       waits=W(gd), incs=[(sd, 1)])


def ph1_front(nc, d, P, S, R_, sc):
    op = P.op
    ld, st, td, vd, sd, gd = S["ld"], S["st"], S["td"], S["vd"], S["sd"], S["gd"]
    s_Ac, s_dtwT = R_["Ac"], R_["dtwT"]

    s_winT = sc.sb("winT", (128, 8 * 2 * DH), BF16)
    s_xT = sc.sb("xTc", (128, 8, 512), BF16)
    s_xTf = sc.sb("xTf", (128, 8, 512))
    s_xbuf = sc.sb("xbuf", (128, NT, 516))
    s_ubuf = sc.sb("ubuf", (128, NT, 512))
    s_zbuf = sc.sb("zbuf", (128, NT, 512))
    s_convw = sc.sb("sb_convw", (128, NT * 4))
    s_convb = sc.sb("sb_convb", (128, NT))
    s_xpT = sc.sb("xpT", (128, NT, E96))
    s_xdbl = sc.sb("sb_xdbl", (R, L))
    s_xdblbc = sc.sb("sb_xdblbc", (2 * N, L))
    s_wf = sc.sb("wstage", (128, 2 * DH))
    p_e = sc.psum("p_e", (128, 512))
    p_x = sc.psum("p_x", (96, 512))

    op("sync", lambda e: e.dma_start(out=s_convw[:], in_=d["conv_w"]), incs=[(ld, 16)])
    op("sync", lambda e: e.dma_start(out=s_convb[:], in_=d["conv_b"]), incs=[(ld, 16)])
    op("sync", lambda e: e.dma_start(out=s_xpT[:],
                                     in_=d["xproj_T"].rearrange("(t p) e -> p t e", p=128)),
       incs=[(ld, 16)])
    op("vector", lambda e: e.memset(s_xbuf[:, :, 0:4], 0.0), incs=[(vd, 1)])

    for k in range(8):
        op("sync", lambda e, k=k: e.dma_start(
            out=s_wf[:], in_=d["w_in_T"][k * 128:(k + 1) * 128, :]),
           waits=W(sd), incs=[(ld, 16)])
        op("scalar", lambda e, k=k: e.copy(
            out=s_winT[:, k * 2 * DH:(k + 1) * 2 * DH], in_=s_wf[:]),
           waits=W(ld), incs=[(sd, 1)])
    op("sync", lambda e: e.dma_start(out=s_wf[0:R, 0:DH], in_=d["dtoutw_T"]),
       waits=W(sd), incs=[(ld, 16)])
    op("scalar", lambda e: e.copy(out=s_dtwT[:], in_=s_wf[0:R, 0:DH]),
       waits=W(ld), incs=[(sd, 1)])

    LC = 512
    xT_r = d["xT"].rearrange("(k p) l -> p k l", p=128)
    u_r = d["u_sp"].rearrange("(t p) l -> p t l", p=128)
    z_r = d["z_sp"].rearrange("(t p) l -> p t l", p=128)
    for lc in range(L // LC):
        for k in range(8):
            op("sync", lambda e, k=k, lc=lc: e.dma_start(
                out=s_xTf[:, k, :], in_=d["xT"][k * 128:(k + 1) * 128,
                                                lc * LC:(lc + 1) * LC]),
               waits=W(sd), incs=[(ld, 16)])
        op("scalar", lambda e: e.copy(out=s_xT[:], in_=s_xTf[:]),
           waits=W(ld), incs=[(sd, 1)])

        for et in range(2 * NT):
            op("tensor", lambda e, et=et: [
                e.matmul(p_e[:], s_winT[:, k * 2 * DH + et * 128:
                                        k * 2 * DH + (et + 1) * 128],
                         s_xT[:, k, :], start=(k == 0), stop=(k == 7))
                for k in range(8)][-1],
               waits=W(sd, vd), incs=[(td, 1)])
            if et < NT:
                op("vector", lambda e, et=et: e.tensor_copy(
                    out=s_xbuf[:, et, 4:516], in_=p_e[:]),
                   waits=W(td), incs=[(vd, 1)])
            else:
                zt = et - NT
                op("vector", lambda e, zt=zt: e.tensor_copy(
                    out=s_zbuf[:, zt, :], in_=p_e[:]),
                   waits=W(td), incs=[(vd, 1)])

        for t in range(NT):
            op("sync", lambda e, t=t, lc=lc: e.dma_start(
                out=d["z_sp"][t * 128:(t + 1) * 128, lc * LC:(lc + 1) * LC],
                in_=s_zbuf[:, t, :]),
               waits=W(vd), incs=[(st, 16)])

        for t in range(NT):
            def conv_f(e, t=t):
                e.tensor_scalar(
                    out=s_ubuf[:, t, :], in0=s_xbuf[:, t, 1:513],
                    scalar1=s_convw[:, t * 4:t * 4 + 1], scalar2=None,
                    op0=ALU.mult)
                for tap in range(1, 4):
                    e.scalar_tensor_tensor(
                        out=s_ubuf[:, t, :], in0=s_xbuf[:, t, 1 + tap:513 + tap],
                        scalar=s_convw[:, t * 4 + tap:t * 4 + tap + 1],
                        in1=s_ubuf[:, t, :], op0=ALU.mult, op1=ALU.add)
                return e.tensor_copy(out=s_xbuf[:, t, 0:4], in_=s_xbuf[:, t, 512:516])
            op("vector", conv_f, waits=W(vd), incs=[(vd, 1)])
            op("scalar", lambda e, t=t: e.activation(
                out=s_ubuf[:, t, :], in_=s_ubuf[:, t, :], func=AF.Silu,
                bias=s_convb[:, t:t + 1]),
               waits=W(vd), incs=[(sd, 1)])

        for t in range(NT):
            op("sync", lambda e, t=t, lc=lc: e.dma_start(
                out=d["u_sp"][t * 128:(t + 1) * 128, lc * LC:(lc + 1) * LC],
                in_=s_ubuf[:, t, :]),
               waits=W(sd), incs=[(st, 16)])
        op("tensor", lambda e: [
            e.matmul(p_x[:], s_xpT[:, t, :], s_ubuf[:, t, :],
                     start=(t == 0), stop=(t == NT - 1))
            for t in range(NT)][-1],
           waits=W(sd, vd), incs=[(td, 1)])
        op("vector", lambda e, lc=lc: e.tensor_copy(
            out=s_xdbl[:, lc * LC:(lc + 1) * LC], in_=p_x[0:R, :]),
           waits=W(td), incs=[(vd, 1)])
        op("vector", lambda e, lc=lc: e.tensor_copy(
            out=s_xdblbc[:, lc * LC:(lc + 1) * LC], in_=p_x[R:E96, :]),
           waits=W(td), incs=[(vd, 1)])

    if DEBUG:
        op("sync", lambda e: e.dma_start(out=d["dbg_u"], in_=d["u_sp"][:, 0:512]),
           waits=W(st), incs=[(st, 16)])
        op("sync", lambda e: e.dma_start(out=d["dbg_z"], in_=d["z_sp"][:, 0:512]),
           waits=W(st), incs=[(st, 16)])

    op("sync", lambda e: e.dma_start(out=d["xdbl_i"][0:R, :], in_=s_xdbl[:]),
       waits=W(vd), incs=[(st, 16)])
    op("sync", lambda e: e.dma_start(out=d["xdbl_i"][R:E96, :], in_=s_xdblbc[:]),
       waits=W(vd), incs=[(st, 16)])
    op("gpsimd", lambda e: e.collective_compute(
        "AllReduce", ALU.add, replica_groups=[[0, 1], [2, 3], [4, 5], [6, 7]],
        ins=[d["xdbl_i"]], outs=[d["xdbl_o"]]),
       waits=W(st), incs=[(gd, 1)])
    op("sync", lambda e: e.dma_start(out=s_xdbl[:], in_=d["xdbl_o"][0:R, :]),
       waits=W(gd), incs=[(ld, 16)])
    op("sync", lambda e: e.dma_start(out=s_xdblbc[:], in_=d["xdbl_o"][R:E96, :]),
       waits=W(gd), incs=[(ld, 16)])
    if DEBUG:
        op("sync", lambda e: e.dma_start(out=d["dbg_xdbl"][0:R, :], in_=s_xdbl[:]),
           waits=W(ld), incs=[(st, 16)])
        op("sync", lambda e: e.dma_start(out=d["dbg_xdbl"][R:E96, :], in_=s_xdblbc[:]),
           waits=W(ld), incs=[(st, 16)])


def ph2_inner(nc, d, P, S, R_, sc):
    op = P.op
    ld, st, td, vd, sd, gd = S["ld"], S["st"], S["td"], S["vd"], S["sd"], S["gd"]
    ld2 = S["ld2"]
    s_dtm, s_ones = R_["dtm"], R_["ones"]

    LC = 2048
    NCH = L // LC
    s_minwT = sc.sb("minwT", (R, 2 * MDI), BF16)
    s_mxpT = sc.sb("mxpT", (MDI, 12))
    s_mdtwT = sc.sb("mdtwT", (MN, MDI))
    s_mcw = sc.sb("mcw", (MDI, MK))
    s_mcb = sc.sb("mcb", (MDI, 1))
    s_mdtb = sc.sb("mdtb", (MDI, 1))
    s_mA = sc.sb("sb_mA", (MDI, MN))
    s_mD = sc.sb("sb_mD", (MDI, 1))
    s_moutwT = sc.sb("moutwT", (MDI, R), BF16)
    s_wst = sc.sb("wst2", (128, 128))
    s_dtin = sc.sb("dtin", (R, LC), BF16)
    s_xm = sc.sb("xm", (MDI, LC + 1))
    s_zm = sc.sb("zm", (MDI, LC))
    s_um = sc.sb("um", (MDI, LC))
    s_xdm = sc.sb("xdm", (12, LC))
    s_dtmm = sc.sb("dtmm", (MDI, LC))
    s_gm = sc.sb("gm", (MDI, LC))
    s_sc1 = sc.sb("msc1", (MDI, LC))
    s_sc2 = sc.sb("msc2", (MDI, LC))
    s_sn = sc.sb("msn", (MDI, LC))
    s_ym = sc.sb("mym", (MDI, LC))
    s_ypm = sc.sb("mypm", (MDI, LC), BF16)
    s_mst = sc.sb("mstates", (MDI, MN))
    s_msel = sc.sb("msel", (12, 12, MDI))
    p_m = sc.psum("p_m", (128, 512))
    p_r = sc.psum("p_r", (MDI, 2048))

    op("sync", lambda e: e.dma_start(out=s_wst[0:R, 0:2 * MDI], in_=d["m_in_wT"]),
       incs=[(ld, 16)])
    op("scalar", lambda e: e.copy(out=s_minwT[:], in_=s_wst[0:R, 0:2 * MDI]),
       waits=W(ld), incs=[(sd, 1)])
    op("sync", lambda e: e.dma_start(out=s_wst[0:MDI, 0:R], in_=d["m_outwT"]),
       waits=W(sd), incs=[(ld, 16)])
    op("scalar", lambda e: e.copy(out=s_moutwT[:], in_=s_wst[0:MDI, 0:R]),
       waits=W(ld), incs=[(sd, 1)])
    op("sync", lambda e: e.dma_start(out=s_mxpT[:], in_=d["m_xp_T"]), incs=[(ld, 16)])
    op("sync", lambda e: e.dma_start(out=s_mdtwT[:], in_=d["m_dt_wT"]), incs=[(ld, 16)])
    op("sync", lambda e: e.dma_start(out=s_mcw[:], in_=d["m_conv_w"]), incs=[(ld, 16)])
    op("sync", lambda e: e.dma_start(out=s_mcb[:], in_=d["m_conv_b"]), incs=[(ld, 16)])
    op("sync", lambda e: e.dma_start(out=s_mdtb[:], in_=d["m_dt_b"]), incs=[(ld, 16)])
    op("sync", lambda e: e.dma_start(out=s_mA[:], in_=d["mA"]), incs=[(ld, 16)])
    op("sync", lambda e: e.dma_start(out=s_mD[:], in_=d["m_D"]), incs=[(ld, 16)])
    op("scalar", lambda e: e.activation(out=s_mA[:], in_=s_mA[:], func=AF.Exp),
       waits=W(ld), incs=[(sd, 1)])
    op("vector", lambda e: e.tensor_scalar(out=s_mA[:], in0=s_mA[:], scalar1=-1.0,
                                           scalar2=None, op0=ALU.mult),
       waits=W(sd), incs=[(vd, 1)])
    op("vector", lambda e: e.memset(s_mst[:], 0.0), incs=[(vd, 1)])
    op("vector", lambda e: e.memset(s_msel[:], 1.0), incs=[(vd, 1)])
    for jj in range(12):
        op("gpsimd", lambda e, jj=jj: e.affine_select(
            out=s_msel[:, jj, :], in_=s_msel[:, jj, :], pattern=[[0, MDI]],
            base=-jj, channel_multiplier=1, compare_op=ALU.is_equal, fill=0.0),
           waits=W(vd), incs=[(gd, 1)])
    op("vector", lambda e: e.memset(s_xm[:, 0:1], 0.0), incs=[(vd, 1)])

    for c in range(NCH):
        lsl = slice(c * LC, (c + 1) * LC)
        op("gpsimd", lambda e, lsl=lsl: e.dma_start(
            out=s_dtin[:], in_=d["xdbl_o"][0:R, lsl]),
           waits=W(vd), incs=[(ld2, 16)])
        for j in range(LC // 512):
            op("tensor", lambda e, j=j: e.matmul(
                p_m[:], s_minwT[:], s_dtin[:, j * 512:(j + 1) * 512],
                start=True, stop=True),
               waits=W(sd, vd, ld2), incs=[(td, 1)])
            op("vector", lambda e, j=j: e.tensor_copy(
                out=s_xm[0:MDI, 1 + j * 512:1 + (j + 1) * 512], in_=p_m[0:MDI, :]),
               waits=W(td), incs=[(vd, 1)])
            op("vector", lambda e, j=j: e.tensor_copy(
                out=s_zm[:, j * 512:(j + 1) * 512], in_=p_m[MDI:128, :]),
               waits=W(td), incs=[(vd, 1)])
        if INNER_UPTO == "inproj":
            continue
        op("vector", lambda e: [
            e.tensor_scalar(out=s_um[:], in0=s_xm[0:MDI, 0:LC],
                            scalar1=s_mcw[:, 0:1], scalar2=None, op0=ALU.mult),
            e.scalar_tensor_tensor(out=s_um[:], in0=s_xm[0:MDI, 1:LC + 1],
                                   scalar=s_mcw[:, 1:2], in1=s_um[:],
                                   op0=ALU.mult, op1=ALU.add),
            e.tensor_copy(out=s_xm[:, 0:1], in_=s_xm[:, LC:LC + 1])][-1],
           waits=W(vd), incs=[(vd, 1)])
        op("scalar", lambda e: e.activation(out=s_um[:], in_=s_um[:], func=AF.Silu,
                                            bias=s_mcb[:, 0:1]),
           waits=W(vd), incs=[(sd, 1)])
        if INNER_UPTO == "conv":
            continue
        for j in range(LC // 512):
            op("tensor", lambda e, j=j: e.matmul(
                p_m[0:12, :], s_mxpT[:], s_um[:, j * 512:(j + 1) * 512],
                start=True, stop=True),
               waits=W(sd, vd), incs=[(td, 1)])
            op("vector", lambda e, j=j: e.tensor_copy(
                out=s_xdm[:, j * 512:(j + 1) * 512], in_=p_m[0:12, :]),
               waits=W(td), incs=[(vd, 1)])
        if INNER_UPTO == "xproj":
            continue
        for j in range(LC // 512):
            op("tensor", lambda e, j=j: e.matmul(
                p_m[0:MDI, :], s_mdtwT[:], s_xdm[0:MN, j * 512:(j + 1) * 512],
                start=True, stop=True),
               waits=W(vd), incs=[(td, 1)])
            op("vector", lambda e, j=j: e.tensor_scalar(
                out=s_dtmm[:, j * 512:(j + 1) * 512], in0=p_m[0:MDI, :],
                scalar1=s_mdtb[:, 0:1], scalar2=80.0, op0=ALU.add, op1=ALU.min),
               waits=W(td), incs=[(vd, 1)])
        op("scalar", lambda e: [
            e.activation(out=s_dtmm[:], in_=s_dtmm[:], func=AF.Exp),
            e.activation(out=s_dtmm[:], in_=s_dtmm[:], func=AF.Ln, bias=1.0)][-1],
           waits=W(vd), incs=[(sd, 1)])
        if INNER_UPTO == "delta":
            continue
        op("vector", lambda e: e.tensor_tensor(out=s_gm[:], in0=s_dtmm[:], in1=s_um[:],
                                               op=ALU.mult),
           waits=W(sd), incs=[(vd, 1)])
        op("vector", lambda e: e.memset(s_ym[:], 0.0), incs=[(vd, 1)])
        for n in range(MN):
            op("scalar", lambda e, n=n: e.activation(
                out=s_sc1[:], in_=s_dtmm[:], func=AF.Exp, scale=s_mA[:, n:n + 1]),
               waits=W(vd), incs=[(sd, 1)])
            op("tensor", lambda e, n=n: [e.matmul(
                p_r[:, j * 512:(j + 1) * 512], s_msel[:, MN + n, :],
                s_xdm[:, j * 512:(j + 1) * 512],
                start=True, stop=True) for j in range(4)][-1],
               waits=W(vd, gd), incs=[(td, 1)])
            op("vector", lambda e: e.tensor_tensor(out=s_sc2[:], in0=s_gm[:],
                                                   in1=p_r[:], op=ALU.mult),
               waits=W(td, sd), incs=[(vd, 1)])
            op("vector", lambda e, n=n: e.tensor_tensor_scan(
                out=s_sn[:], data0=s_sc1[:], data1=s_sc2[:],
                initial=s_mst[:, n:n + 1], op0=ALU.mult, op1=ALU.add),
               waits=W(vd), incs=[(vd, 1)])
            op("vector", lambda e, n=n: e.tensor_copy(out=s_mst[:, n:n + 1],
                                                      in_=s_sn[:, LC - 1:LC]),
               waits=W(vd), incs=[(vd, 1)])
            op("tensor", lambda e, n=n: [e.matmul(
                p_r[:, j * 512:(j + 1) * 512], s_msel[:, 2 * MN + n, :],
                s_xdm[:, j * 512:(j + 1) * 512],
                start=True, stop=True) for j in range(4)][-1],
               waits=W(vd, gd), incs=[(td, 1)])
            op("vector", lambda e: e.tensor_tensor(out=s_sc2[:], in0=s_sn[:],
                                                   in1=p_r[:], op=ALU.mult),
               waits=W(td), incs=[(vd, 1)])
            op("vector", lambda e: e.tensor_tensor(out=s_ym[:], in0=s_ym[:],
                                                   in1=s_sc2[:], op=ALU.add),
               waits=W(vd), incs=[(vd, 1)])
        if INNER_UPTO == "scan":
            continue
        op("scalar", lambda e: e.activation(out=s_zm[:], in_=s_zm[:], func=AF.Silu),
           waits=W(vd), incs=[(sd, 1)])
        op("vector", lambda e: e.scalar_tensor_tensor(
            out=s_ym[:], in0=s_um[:], scalar=s_mD[:, 0:1], in1=s_ym[:],
            op0=ALU.mult, op1=ALU.add),
           waits=W(sd), incs=[(vd, 1)])
        op("vector", lambda e: e.tensor_tensor(out=s_ypm[:], in0=s_ym[:], in1=s_zm[:],
                                               op=ALU.mult),
           waits=W(vd), incs=[(vd, 1)])
        if INNER_UPTO == "ypost":
            continue
        for j in range(LC // 512):
            op("tensor", lambda e, j=j: e.matmul(
                p_m[0:R, :], s_moutwT[:], s_ypm[:, j * 512:(j + 1) * 512],
                start=True, stop=True),
               waits=W(vd, sd), incs=[(td, 1)])
            op("scalar", lambda e, j=j, c=c: e.copy(
                out=s_dtm[:, c * LC + j * 512:c * LC + (j + 1) * 512],
                in_=p_m[0:R, :]),
               waits=W(td), incs=[(sd, 1)])

    if DEBUG:
        op("sync", lambda e: e.dma_start(out=d["dbg_dtm"], in_=s_dtm[:]),
           waits=W(sd), incs=[(st, 16)])


def ph3_scan(nc, d, P, S, R_, sc):
    op = P.op
    ld, st, td, vd, sd, gd = S["ld"], S["st"], S["td"], S["vd"], S["sd"], S["gd"]
    s_dtm, s_dtwT = R_["dtm"], R_["dtwT"]
    s_Ac, s_Dc, s_dtbc = R_["Ac"], R_["Dc"], R_["dtbc"]

    LCH = 1024
    NCH = L // LCH

    s_dts = sc.sb("dts", (128, NT, LCH))
    s_gs = sc.sb("gs", (128, NT, LCH))
    s_ys = sc.sb("ys", (128, NT, LCH))
    s_sn = sc.sb("ssn", (128, LCH))
    s_u2 = sc.sb("su2", (128, LCH))
    s_z2 = sc.sb("sz2", (128, LCH))
    s_yp = sc.sb("syp", (128, LCH), BF16)
    s_states = sc.sb("states", (128, NT * N))
    s_sel = sc.sb("sel32", (32, 2 * N, 128))
    s_bcc = sc.sb("bcc", (2 * N, LCH))
    p_b = [sc.psum("p_b0", (128, LCH)), sc.psum("p_b1", (128, LCH))]
    p_c = [sc.psum("p_c0", (128, LCH)), sc.psum("p_c1x", (128, LCH))]
    s_da = [sc.sb("sda0", (128, LCH)), sc.sb("sda1", (128, LCH))]
    s_db = [sc.sb("sdb0", (128, LCH)), sc.sb("sdb1", (128, LCH))]

    u_r = d["u_sp"].rearrange("(t p) l -> p t l", p=128)
    z_r = d["z_sp"].rearrange("(t p) l -> p t l", p=128)

    op("vector", lambda e: e.memset(s_sel[:], 1.0), incs=[(vd, 1)])
    for jj in range(2 * N):
        op("gpsimd", lambda e, jj=jj: e.affine_select(
            out=s_sel[:, jj, :], in_=s_sel[:, jj, :], pattern=[[0, 128]],
            base=-jj, channel_multiplier=1, compare_op=ALU.is_equal, fill=0.0),
           waits=W(vd), incs=[(gd, 1)])
    op("vector", lambda e: e.memset(s_states[:], 0.0), incs=[(vd, 1)])
    gsel_done = S["gd"].v
    bev = [None, None]
    cev = [None, None]
    dslot_ev = [None, None]
    eslot_ev = [None, None]
    sn_ev = [None]

    for c in range(NCH):
        lsl = slice(c * LCH, (c + 1) * LCH)
        for t in range(NT):
            op("sync", lambda e, t=t, lsl=lsl: e.dma_start(
                out=s_gs[:, t, :], in_=d["u_sp"][t * 128:(t + 1) * 128, lsl]),
               waits=W(vd, gd), incs=[(ld, 16)])
        op("sync", lambda e, lsl=lsl: e.dma_start(
            out=s_bcc[:], in_=d["xdbl_o"][R:E96, lsl]),
           waits=W(td), incs=[(ld, 16)])

        bcc_ld = (ld.v,)
        for t in range(NT):
            for j in range(LCH // 512):
                op("tensor", lambda e, t=t, j=j, c=c: e.matmul(
                    p_b[0][:, j * 512:(j + 1) * 512],
                    s_dtwT[:, t * 128:(t + 1) * 128],
                    s_dtm[:, c * LCH + j * 512:c * LCH + (j + 1) * 512],
                    start=True, stop=True),
                   waits=W(vd, sd), incs=[(td, 1)])
                op("vector", lambda e, t=t, j=j: e.tensor_scalar(
                    out=s_dts[:, t, j * 512:(j + 1) * 512],
                    in0=p_b[0][:, j * 512:(j + 1) * 512],
                    scalar1=s_dtbc[:, t:t + 1], scalar2=80.0,
                    op0=ALU.add, op1=ALU.min),
                   waits=W(td), incs=[(vd, 1)])
            op("scalar", lambda e, t=t: [
                e.activation(out=s_dts[:, t, :], in_=s_dts[:, t, :], func=AF.Exp),
                e.activation(out=s_dts[:, t, :], in_=s_dts[:, t, :], func=AF.Ln,
                             bias=1.0)][-1],
               waits=W(vd), incs=[(sd, 1)])
            if DEBUG and c == 0:
                op("sync", lambda e, t=t: e.dma_start(
                    out=d["dbg_z"][t * 128:(t + 1) * 128, :],
                    in_=s_gs[:, t, 0:512]),
                   waits=W(ld, sd), incs=[(st, 16)])
                op("vector", lambda e, t=t: e.tensor_tensor(
                    out=s_gs[:, t, :], in0=s_dts[:, t, :], in1=s_gs[:, t, :],
                    op=ALU.mult),
                   waits=W(sd, ld, st), incs=[(vd, 1)])
            else:
                op("vector", lambda e, t=t: e.tensor_tensor(
                    out=s_gs[:, t, :], in0=s_dts[:, t, :], in1=s_gs[:, t, :],
                    op=ALU.mult),
                   waits=W(sd, ld), incs=[(vd, 1)])
        op("vector", lambda e: e.memset(s_ys[:], 0.0), waits=W(gd), incs=[(vd, 1)])
        gdone = vd.v
        bev[0] = (vd, vd.v)

        for n in range(N):
            pslot = n % 2
            wb = [bev[pslot]] if bev[pslot] else []
            op("tensor", lambda e, n=n, c=c, pslot=pslot: [e.matmul(
                p_b[pslot][:, j * 512:(j + 1) * 512], s_sel[:, n, :],
                s_bcc[:, j * 512:(j + 1) * 512],
                start=True, stop=True) for j in range(2)][-1],
               waits=wb + [(gd, gsel_done), (ld, bcc_ld[0])], incs=[(td, 1)])
            bbuilt = (td, td.v)
            wc = [cev[pslot]] if cev[pslot] else []
            op("tensor", lambda e, n=n, c=c, pslot=pslot: [e.matmul(
                p_c[pslot][:, j * 512:(j + 1) * 512], s_sel[:, N + n, :],
                s_bcc[:, j * 512:(j + 1) * 512],
                start=True, stop=True) for j in range(2)][-1],
               waits=wc, incs=[(td, 1)])
            cbuilt = (td, td.v)
            for t in range(NT):
                idx = n * NT + t
                sl = idx % 2
                wda = [dslot_ev[sl]] if dslot_ev[sl] else []
                op("scalar", lambda e, t=t, n=n, sl=sl: e.activation(
                    out=s_da[sl][:], in_=s_dts[:, t, :], func=AF.Exp,
                    scale=s_Ac[:, t * N + n:t * N + n + 1]),
                   waits=wda + [(vd, gdone)], incs=[(sd, 1)])
                da_ready = (sd, sd.v)
                wdb = [eslot_ev[sl]] if eslot_ev[sl] else []
                op("vector", lambda e, t=t, sl=sl, pslot=pslot: e.tensor_tensor(
                    out=s_db[sl][:], in0=s_gs[:, t, :], in1=p_b[pslot][:],
                    op=ALU.mult),
                   waits=wdb + [bbuilt], incs=[(vd, 1)])
                db_ready = (vd, vd.v)
                wsc = [sn_ev[0]] if sn_ev[0] else []
                op("vector", lambda e, t=t, n=n, sl=sl: e.tensor_tensor_scan(
                    out=s_sn[:], data0=s_da[sl][:], data1=s_db[sl][:],
                    initial=s_states[:, t * N + n:t * N + n + 1],
                    op0=ALU.mult, op1=ALU.add),
                   waits=wsc + [da_ready, db_ready], incs=[(vd, 1)])
                dslot_ev[sl] = (vd, vd.v)
                eslot_ev[sl] = (vd, vd.v)
                op("vector", lambda e, t=t, n=n: e.tensor_copy(
                    out=s_states[:, t * N + n:t * N + n + 1],
                    in_=s_sn[:, LCH - 1:LCH]),
                   incs=[(vd, 1)])
                op("vector", lambda e, t=t, pslot=pslot: e.tensor_tensor(
                    out=s_sn[:], in0=s_sn[:], in1=p_c[pslot][:], op=ALU.mult),
                   waits=[cbuilt], incs=[(vd, 1)])
                cev[pslot] = (vd, vd.v)
                w_ready = (vd, vd.v)
                op("gpsimd", lambda e, t=t: e.tensor_tensor(
                    out=s_ys[:, t, :], in0=s_ys[:, t, :], in1=s_sn[:], op=ALU.add),
                   waits=[w_ready], incs=[(gd, 1)])
                sn_ev[0] = (gd, gd.v)
                if t == NT - 1:
                    bev[pslot] = (vd, vd.v)

        for t in range(NT):
            op("sync", lambda e, t=t, lsl=lsl: e.dma_start(
                out=s_u2[:], in_=d["u_sp"][t * 128:(t + 1) * 128, lsl]),
               waits=W(vd), incs=[(ld, 16)])
            op("sync", lambda e, t=t, lsl=lsl: e.dma_start(
                out=s_z2[:], in_=d["z_sp"][t * 128:(t + 1) * 128, lsl]),
               waits=W(sd), incs=[(ld, 16)])
            op("scalar", lambda e: e.activation(out=s_z2[:], in_=s_z2[:],
                                                func=AF.Silu),
               waits=W(ld), incs=[(sd, 1)])
            op("vector", lambda e, t=t: e.scalar_tensor_tensor(
                out=s_ys[:, t, :], in0=s_u2[:], scalar=s_Dc[:, t:t + 1],
                in1=s_ys[:, t, :], op0=ALU.mult, op1=ALU.add),
               waits=W(ld, sd, gd), incs=[(vd, 1)])
            op("vector", lambda e, t=t: e.tensor_tensor(
                out=s_yp[:], in0=s_ys[:, t, :], in1=s_z2[:], op=ALU.mult),
               waits=W(vd), incs=[(vd, 1)])
            op("sync", lambda e, t=t, c=c: e.dma_start(
                out=d["yp_i"][t * 128:(t + 1) * 128, c * LCH:(c + 1) * LCH],
                in_=s_yp[:]),
               waits=W(vd), incs=[(st, 16)])

    if DEBUG:
        op("sync", lambda e: e.dma_start(out=d["dbg_ypost"], in_=d["yp_i"]),
           waits=W(st), incs=[(st, 16)])

    op("gpsimd", lambda e: e.collective_compute(
        "AllGather", ALU.bypass,
        replica_groups=[[0, 1], [2, 3], [4, 5], [6, 7]],
        ins=[d["yp_i"]], outs=[d["yp_o"]]),
       waits=W(st), incs=[(gd, 1)])


def ph4_tail(nc, d, P, S, R_, sc):
    op = P.op
    ld, st, td, vd, sd, gd = S["ld"], S["st"], S["td"], S["vd"], S["sd"], S["gd"]
    ld2 = S["ld2"]
    s_ones, s_id, s_idb = R_["ones"], R_["id"], R_["idb"]
    NTT = LT // 128

    s_wqs = sc.sb("wqs", (128, 384))
    s_bm = sc.sb("bm", (128, 12))
    s_bs = sc.sb("bs", (128, 12))
    s_br = sc.sb("br", (128, 12))
    s_wqo = sc.sb("wqo", (128, 384), BF16)
    s_stg = sc.sb("stg", (128, DM))
    s_x1 = sc.sb("x1t", (128, DM))
    s_sq = sc.sb("sqt", (128, DM))
    s_c1 = sc.sb("c1", (128, 1))
    s_c2 = sc.sb("c2", (128, 1))
    s_c3 = sc.sb("c3", (128, 1))
    s_mx = sc.sb("mxc", (128, 4))
    s_gmax = sc.sb("gmax", (1, 4))
    s_sx = sc.sb("sx", (128, 2))
    s_su = sc.sb("su", (128, 8))
    s_f1 = sc.sb("f1", (128, FI))
    s_f2 = sc.sb("f2", (128, FI))
    s_hoff = sc.sb("hoffs", (1, 1), mybir.dt.int32)
    s_red = sc.sb("red", (4, 1))
    p_trb = sc.psum("p_trb", (128, 128), BF16)
    p_tr2 = sc.psum("p_tr2", (128, 128))
    p_o = sc.psum("p_o", (128, 1024))
    p_c1 = sc.psum("p_c1", (128, 4))

    def qdq_weight(name_in, rows, blocks, out_ap_of, tile_cols):
        for ot in range(rows):
            op("sync", lambda e, ot=ot, tc=tile_cols: e.dma_start(
                out=s_wqs[:, 0:tc],
                in_=d[name_in][ot * 128:(ot + 1) * 128, :]),
               waits=W(st, vd), incs=[(ld, 16)])
            op("vector", lambda e, b=blocks, tc=tile_cols: e.reduce_max(
                out=s_bm[:, 0:b],
                in_=s_wqs[:, 0:tc].rearrange("p (b k) -> p b k", k=32),
                axis=AX.X, apply_absolute_value=True),
               waits=W(ld), incs=[(vd, 1)])
            op("vector", lambda e, b=blocks: e.tensor_scalar(
                out=s_bs[:, 0:b], in0=s_bm[:, 0:b], scalar1=1.0 / 7.0,
                scalar2=1e-12, op0=ALU.mult, op1=ALU.max),
               waits=W(vd), incs=[(vd, 1)])
            op("vector", lambda e, b=blocks: e.reciprocal(
                out=s_br[:, 0:b], in_=s_bs[:, 0:b]),
               waits=W(vd), incs=[(vd, 1)])
            op("vector", lambda e, b=blocks, tc=tile_cols: e.tensor_tensor(
                out=s_wqs[:, 0:tc].rearrange("p (b k) -> p b k", k=32),
                in0=s_wqs[:, 0:tc].rearrange("p (b k) -> p b k", k=32),
                in1=s_br[:, 0:b].rearrange("p (b o) -> p b o", o=1).broadcast_to(
                    (128, b, 32)),
                op=ALU.mult),
               waits=W(vd), incs=[(vd, 1)])
            op("vector", lambda e, tc=tile_cols: e.tensor_scalar(
                out=s_wqs[:, 0:tc], in0=s_wqs[:, 0:tc], scalar1=MAGIC,
                scalar2=MAGIC, op0=ALU.add, op1=ALU.subtract),
               waits=W(vd), incs=[(vd, 1)])
            op("vector", lambda e, b=blocks, tc=tile_cols: e.tensor_tensor(
                out=s_wqo[:, 0:tc].rearrange("p (b k) -> p b k", k=32),
                in0=s_wqs[:, 0:tc].rearrange("p (b k) -> p b k", k=32),
                in1=s_bs[:, 0:b].rearrange("p (b o) -> p b o", o=1).broadcast_to(
                    (128, b, 32)),
                op=ALU.mult),
               waits=W(vd), incs=[(vd, 1)])
            for j in range(tile_cols // 128):
                op("tensor", lambda e, j=j: e.transpose(
                    p_trb[:], s_wqo[:, j * 128:(j + 1) * 128], s_idb[:]),
                   waits=W(vd, st, gd), incs=[(td, 1)])
                op("vector", lambda e, j=j: e.tensor_copy(
                    out=s_wqo[:, j * 128:(j + 1) * 128], in_=p_trb[:]),
                   waits=W(td), incs=[(vd, 1)])
                op("sync", lambda e, ot=ot, j=j: e.dma_start(
                    out=out_ap_of(ot, j), in_=s_wqo[:, j * 128:(j + 1) * 128]),
                   waits=W(vd), incs=[(st, 16)])

    qdq_weight("wg_s", 24, 4,
               lambda ot, j: d["wgT_i"][:, ot * 128:(ot + 1) * 128], 128)
    qdq_weight("wu_s", 24, 4,
               lambda ot, j: d["wuT_i"][:, ot * 128:(ot + 1) * 128], 128)
    qdq_weight("wd_s", 8, 12,
               lambda ot, j: d["wdT_i"][j * 128:(j + 1) * 128,
                                        ot * 128:(ot + 1) * 128], 384)

    for nm in ("wg", "wu", "wd"):
        op("gpsimd", lambda e, nm=nm: e.collective_compute(
            "AllGather", ALU.bypass,
            replica_groups=[[0, 1, 2, 3, 4, 5, 6, 7]],
            ins=[d[f"{nm}T_i"]], outs=[d[f"{nm}T_o"]]),
           waits=W(st), incs=[(gd, 1)])

    if TAIL_UPTO == "qdq":
        return
    scA = type(sc)()
    s_ow = scA.sb("ow", (128, 16, DM), BF16)
    for k in range(16):
        op("sync", lambda e, k=k: e.dma_start(
            out=s_stg[:], in_=d["outw_T"][k * 128:(k + 1) * 128, :]),
           waits=W(sd), incs=[(ld, 16)])
        op("scalar", lambda e, k=k: e.copy(out=s_ow[:, k, :], in_=s_stg[:]),
           waits=W(ld), incs=[(sd, 1)])
    ow_ev = (sd, sd.v)

    s_ypt = scA.sb("ypt", (128, 16, 1024), BF16)
    p_o2 = [p_o, scA.psum("p_oB", (128, 1024))]
    s_stg2 = [scA.sb("stgA", (128, DM)), scA.sb("stgB", (128, DM))]
    s_x12 = [scA.sb("x1A", (128, DM)), scA.sb("x1B", (128, DM))]
    s_x1t2 = [scA.sb("x1tA", (128, DM)), scA.sb("x1tB", (128, DM))]
    s_sq2 = [scA.sb("sqA", (128, DM)), scA.sb("sqB", (128, DM))]
    s_c12 = [scA.sb("c1A", (128, 1)), scA.sb("c1B", (128, 1))]
    s_c22 = [scA.sb("c2A", (128, 1)), scA.sb("c2B", (128, 1))]
    s_c32 = [scA.sb("c3A", (128, 1)), scA.sb("c3B", (128, 1))]
    stg_free = [None, None]
    H = {}
    op("sync", lambda e: e.dma_start(out=s_hoff[:], in_=d["hoff"]),
       incs=[(ld, 16)])

    def _loadh(e):
        e.wait_ge(S["ld"].h, _loadh.ldv)
        v = e.value_load(s_hoff[0:1, 0:1])
        H["v"] = e.snap(v, min_val=0, max_val=1)
        return e.nop()
    _loadh.ldv = S["ld"].v
    op("gpsimd", _loadh)
    op("vector", lambda e: e.memset(s_mx[:], 0.0), incs=[(vd, 1)])

    def layernorm_tile(psrc, out_sb, track_max_col, c1, c2, c3, sqb, stgb, x1b,
                       add_wait, dbg=False):
        op("vector", lambda e: e.tensor_tensor(
            out=x1b[:], in0=psrc, in1=stgb[:], op=ALU.add),
           waits=add_wait, incs=[(vd, 1)])
        xadd = (vd, vd.v)
        op("scalar", lambda e: e.activation(out=sqb[:], in_=x1b[:],
                                            func=AF.Square),
           waits=[xadd], incs=[(sd, 1)])
        sq_ev = (sd, sd.v)
        op("vector", lambda e: e.reduce_sum(out=c1[:], in_=x1b[:], axis=AX.X),
           waits=W(vd), incs=[(vd, 1)])
        op("vector", lambda e: e.reduce_sum(out=c2[:], in_=sqb[:], axis=AX.X),
           waits=list(W(vd)) + [sq_ev], incs=[(vd, 1)])
        op("vector", lambda e: e.tensor_scalar(
            out=c1[:], in0=c1[:], scalar1=1.0 / DM, scalar2=None, op0=ALU.mult),
           waits=W(vd), incs=[(vd, 1)])
        op("vector", lambda e: e.tensor_scalar(
            out=c2[:], in0=c2[:], scalar1=1.0 / DM, scalar2=None, op0=ALU.mult),
           waits=W(vd), incs=[(vd, 1)])
        op("vector", lambda e: e.scalar_tensor_tensor(
            out=c3[:], in0=c1[:], scalar=c1[:, 0:1], in1=c2[:], op0=ALU.mult,
            op1=ALU.subtract),
           waits=W(vd), incs=[(vd, 1)])
        op("vector", lambda e: e.tensor_scalar(
            out=c3[:], in0=c3[:], scalar1=-1.0, scalar2=1e-5, op0=ALU.mult,
            op1=ALU.add),
           waits=W(vd), incs=[(vd, 1)])
        vprep = (vd, vd.v)
        op("scalar", lambda e: e.activation(out=c3[:], in_=c3[:], func=AF.Sqrt),
           waits=[vprep], incs=[(sd, 1)])
        rt = (sd, sd.v)
        op("vector", lambda e: e.reciprocal(out=c3[:], in_=c3[:]),
           waits=list(W(vd)) + [rt], incs=[(vd, 1)])
        op("vector", lambda e: e.tensor_scalar(
            out=out_sb, in0=x1b[:], scalar1=c1[:, 0:1], scalar2=c3[:, 0:1],
            op0=ALU.subtract, op1=ALU.mult),
           waits=W(vd), incs=[(vd, 1)])
        fin = (vd, vd.v)
        if track_max_col is not None:
            op("vector", lambda e: e.reduce_max(
                out=c2[:], in_=out_sb, axis=AX.X, apply_absolute_value=True),
               waits=W(vd), incs=[(vd, 1)])
            op("vector", lambda e, tc=track_max_col: e.tensor_tensor(
                out=s_mx[:, tc:tc + 1], in0=s_mx[:, tc:tc + 1], in1=c2[:],
                op=ALU.max),
               waits=W(vd), incs=[(vd, 1)])
        return xadd, fin

    op("gpsimd", lambda e: e.dma_start(
        out=d["yp_my"], in_=d["yp_o"][:, bass.ts(H["v"], LT)]),
       waits=W(gd), incs=[(ld2, 16)])
    yst = (ld2, ld2.v)
    mm_done = [None, None]
    ln_psum_free = [None, None]
    oct_done = None
    for lt in range(NTT):
        sl = lt % 2
        if lt % 8 == 0:
            wo = [oct_done] if oct_done else []
            for k in range(16):
                op("sync", lambda e, o=lt // 8, k=k: e.dma_start(
                    out=s_ypt[:, k, :],
                    in_=d["yp_my"][k * 128:(k + 1) * 128, o * 1024:(o + 1) * 1024]),
                   waits=wo + [yst], incs=[(ld, 16)])
            ypt_ev = (ld, ld.v)
        co = (lt % 8) * 128
        wmm = ([ln_psum_free[sl]] if ln_psum_free[sl] else []) + [ypt_ev, ow_ev]
        op("tensor", lambda e, co=co, sl=sl: [
            e.matmul(p_o2[sl][:, jh * 512:(jh + 1) * 512], s_ypt[:, k, co:co + 128],
                     s_ow[:, k, jh * 512:(jh + 1) * 512],
                     start=(k == 0), stop=(k == 15))
            for jh in range(2) for k in range(16)][-1],
           waits=wmm, incs=[(td, 1)])
        mm_done[sl] = (td, td.v)
        if lt >= 7:
            oct_done = mm_done[sl]
        wst = [stg_free[sl]] if stg_free[sl] else []
        op("sync", lambda e, lt=lt, sl=sl: e.dma_start(
            out=s_stg2[sl][:], in_=d["x_nat"][lt * 128:(lt + 1) * 128, :]),
           waits=wst, incs=[(ld, 16)])
        xn_ev = (ld, ld.v)
        xadd, fin = layernorm_tile(p_o2[sl][:], s_x12[sl][:], 0, s_c12[sl],
                                   s_c22[sl], s_c32[sl], s_sq2[sl],
                                   s_stg2[sl], s_x1t2[sl],
                                   [mm_done[sl], xn_ev],
                                   dbg=(DEBUG and lt == 0))
        ln_psum_free[sl] = xadd
        stg_free[sl] = xadd
        op("sync", lambda e, lt=lt, sl=sl: e.dma_start(
            out=d["x1_sp"][lt * 128:(lt + 1) * 128, :], in_=s_x12[sl][:]),
           waits=[fin], incs=[(st, 16)])
        if DEBUG:
            op("sync", lambda e, lt=lt, sl=sl: e.dma_start(
                out=d["dbg_x1"][lt * 128:(lt + 1) * 128, :], in_=s_x12[sl][:]),
               waits=[fin], incs=[(st, 16)])
        if DEBUG and lt == 0:
            op("gpsimd", lambda e: e.dma_start(
                out=d["dbg_y"][0:128, :], in_=s_ypt[:, 0, :]),
               waits=[ypt_ev], incs=[(st, 16)])
            op("gpsimd", lambda e: e.dma_start(
                out=d["dbg_y"][128:256, :], in_=d["yp_my"][0:128, 0:1024]),
               waits=[yst], incs=[(st, 16)])
            op("gpsimd", lambda e: e.dma_start(
                out=d["dbg_y"][256:384, :], in_=s_stg2[0][:, 0:1024]),
               waits=[xn_ev], incs=[(st, 16)])
            op("gpsimd", lambda e: e.dma_start(
                out=d["dbg_y"][384:512, :], in_=s_x1t2[0][:]),
               waits=[fin], incs=[(st, 16)])
            op("gpsimd", lambda e: e.dma_start(
                out=d["dbg_y"][512:640, :], in_=s_ow[:, 0, :]),
               waits=[ow_ev], incs=[(st, 16)])
            op("gpsimd", lambda e: e.dma_start(
                out=d["dbg_y"][640:768, 0:1], in_=s_c12[0][:],
                allow_slow_non_contiguous=True),
               waits=[fin], incs=[(st, 16)])
            op("gpsimd", lambda e: e.dma_start(
                out=d["dbg_y"][640:768, 1:2], in_=s_c32[0][:],
                allow_slow_non_contiguous=True),
               waits=[fin], incs=[(st, 16)])
            op("gpsimd", lambda e: e.dma_start(
                out=d["dbg_y"][768:896, 0:1], in_=s_c22[0][:],
                allow_slow_non_contiguous=True),
               waits=[fin], incs=[(st, 16)])

    op("tensor", lambda e: e.transpose(p_tr2[0:1, :], s_mx[:, 0:1], s_id[:]),
       waits=W(vd), incs=[(td, 1)])
    op("vector", lambda e: e.reduce_max(out=s_red[0:1, :], in_=p_tr2[0:1, :],
                                        axis=AX.X),
       waits=W(td), incs=[(vd, 1)])
    op("sync", lambda e: e.dma_start(out=d["mx1_i"], in_=s_red[0:1, :]),
       waits=W(vd), incs=[(st, 16)])
    op("gpsimd", lambda e: e.collective_compute(
        "AllReduce", ALU.max, replica_groups=[[0, 1, 2, 3, 4, 5, 6, 7]],
        ins=[d["mx1_i"]], outs=[d["mx1_o"]]),
       waits=W(st), incs=[(gd, 1)])
    op("sync", lambda e: e.dma_start(out=s_gmax[0:1, 0:1], in_=d["mx1_o"]),
       waits=W(gd), incs=[(ld, 16)])
    op("tensor", lambda e: e.matmul(p_c1[:, 0:1], s_ones[0:1, :],
                                    s_gmax[0:1, 0:1], start=True, stop=True),
       waits=W(ld), incs=[(td, 1)])
    op("vector", lambda e: e.tensor_scalar(
        out=s_sx[:, 0:1], in0=p_c1[:, 0:1], scalar1=1.0 / 32767.0,
        scalar2=1e-12, op0=ALU.mult, op1=ALU.max),
       waits=W(td), incs=[(vd, 1)])
    op("vector", lambda e: e.reciprocal(out=s_sx[:, 1:2], in_=s_sx[:, 0:1]),
       waits=W(vd), incs=[(vd, 1)])
    P.emit(); P.clear()
    scA.close()

    if TAIL_UPTO == "outproj":
        return
    scB = type(sc)()
    s_wg = scB.sb("wgr", (128, 8, FI), BF16)
    s_wu = scB.sb("wur", (128, 8, FI), BF16)
    for k in range(8):
        op("sync", lambda e, k=k: e.dma_start(
            out=s_wg[:, k, :], in_=d["wgT_o"][k * 128:(k + 1) * 128, :]),
           waits=W(gd), incs=[(ld, 16)])
        op("sync", lambda e, k=k: e.dma_start(
            out=s_wu[:, k, :], in_=d["wuT_o"][k * 128:(k + 1) * 128, :]),
           waits=W(gd), incs=[(ld, 16)])

    s_x1q = scB.sb("x1q", (128, DM), BF16)
    s_x1qT = scB.sb("x1qT", (128, 8, 128), BF16)
    p_g = scB.psum("p_gg", (128, 1024))

    for lt in range(NTT):
        op("sync", lambda e, lt=lt: e.dma_start(
            out=s_stg[:], in_=d["x1_sp"][lt * 128:(lt + 1) * 128, :]),
           waits=W(td, vd), incs=[(ld, 16)])
        op("vector", lambda e: e.tensor_scalar(
            out=s_x1[:], in0=s_stg[:], scalar1=s_sx[:, 1:2], scalar2=None,
            op0=ALU.mult),
           waits=W(ld), incs=[(vd, 1)])
        op("vector", lambda e: e.tensor_scalar(
            out=s_x1[:], in0=s_x1[:], scalar1=MAGIC, scalar2=MAGIC,
            op0=ALU.add, op1=ALU.subtract),
           waits=W(vd), incs=[(vd, 1)])
        op("vector", lambda e: e.tensor_scalar(
            out=s_x1q[:], in0=s_x1[:], scalar1=s_sx[:, 0:1], scalar2=None,
            op0=ALU.mult),
           waits=W(vd), incs=[(vd, 1)])
        for j in range(8):
            op("tensor", lambda e, j=j: e.transpose(
                p_trb[:], s_x1q[:, j * 128:(j + 1) * 128], s_idb[:]),
               waits=W(vd), incs=[(td, 1)])
            op("vector", lambda e, j=j: e.tensor_copy(
                out=s_x1qT[:, j, :], in_=p_trb[:]),
               waits=W(td), incs=[(vd, 1)])
        for which, wres, spname, mcol in (
                (0, s_wu, "up_sp", 1), (1, s_wg, "gate_sp", 2)):
            for og in range(3):
                op("tensor", lambda e, wres=wres, og=og: [
                    e.matmul(p_g[:, jh * 512:(jh + 1) * 512], s_x1qT[:, k, :],
                             wres[:, k, og * 1024 + jh * 512:og * 1024 + (jh + 1) * 512],
                             start=(k == 0), stop=(k == 7))
                    for jh in range(2) for k in range(8)][-1],
                   waits=W(vd, ld), incs=[(td, 1)])
                op("vector", lambda e, og=og: e.tensor_copy(
                    out=s_f1[:, og * 1024:(og + 1) * 1024], in_=p_g[:]),
                   waits=W(td, st), incs=[(vd, 1)])
            op("vector", lambda e: e.reduce_max(
                out=s_c2[:], in_=s_f1[:], axis=AX.X, apply_absolute_value=True),
               waits=W(vd), incs=[(vd, 1)])
            op("vector", lambda e, mcol=mcol: e.tensor_tensor(
                out=s_mx[:, mcol:mcol + 1], in0=s_mx[:, mcol:mcol + 1],
                in1=s_c2[:], op=ALU.max),
               waits=W(vd), incs=[(vd, 1)])
            op("sync", lambda e, lt=lt, spname=spname: e.dma_start(
                out=d[spname][lt * 128:(lt + 1) * 128, :], in_=s_f1[:]),
               waits=W(vd), incs=[(st, 16)])

    op("tensor", lambda e: e.transpose(p_tr2[0:2, :], s_mx[:, 1:3], s_id[:]),
       waits=W(vd), incs=[(td, 1)])
    op("vector", lambda e: e.reduce_max(out=s_red[0:2, :], in_=p_tr2[0:2, :],
                                        axis=AX.X),
       waits=W(td), incs=[(vd, 1)])
    op("sync", lambda e: e.dma_start(out=d["mx2_i"], in_=s_red[0:2, :]),
       waits=W(vd, st), incs=[(st, 16)])
    op("gpsimd", lambda e: e.collective_compute(
        "AllReduce", ALU.max, replica_groups=[[0, 1, 2, 3, 4, 5, 6, 7]],
        ins=[d["mx2_i"]], outs=[d["mx2_o"]]),
       waits=W(st), incs=[(gd, 1)])
    op("sync", lambda e: e.dma_start(out=s_gmax[0:1, 0:2], in_=d["mx2_o"]),
       waits=W(gd), incs=[(ld, 16)])
    op("tensor", lambda e: e.matmul(p_c1[:, 0:2], s_ones[0:1, :],
                                    s_gmax[0:1, 0:2], start=True, stop=True),
       waits=W(ld, vd), incs=[(td, 1)])
    op("vector", lambda e: e.tensor_scalar(
        out=s_su[:, 0:2], in0=p_c1[:, 0:2], scalar1=1.0 / 32767.0,
        scalar2=1e-12, op0=ALU.mult, op1=ALU.max),
       waits=W(td), incs=[(vd, 1)])
    op("vector", lambda e: e.reciprocal(out=s_su[:, 2:3], in_=s_su[:, 0:1]),
       waits=W(vd), incs=[(vd, 1)])
    op("vector", lambda e: e.reciprocal(out=s_su[:, 3:4], in_=s_su[:, 1:2]),
       waits=W(vd), incs=[(vd, 1)])
    P.emit(); P.clear()
    scB.close()

    if TAIL_UPTO == "p1":
        return
    op("vector", lambda e: e.memset(s_mx[:, 3:4], 0.0), incs=[(vd, 1)])
    for lt in range(NTT):
        op("sync", lambda e, lt=lt: e.dma_start(
            out=s_f1[:], in_=d["gate_sp"][lt * 128:(lt + 1) * 128, :]),
           waits=W(vd, st), incs=[(ld, 16)])
        op("vector", lambda e: e.tensor_scalar(
            out=s_f1[:], in0=s_f1[:], scalar1=s_su[:, 3:4], scalar2=None,
            op0=ALU.mult),
           waits=W(ld), incs=[(vd, 1)])
        op("vector", lambda e: e.tensor_scalar(
            out=s_f1[:], in0=s_f1[:], scalar1=MAGIC, scalar2=MAGIC,
            op0=ALU.add, op1=ALU.subtract),
           waits=W(vd), incs=[(vd, 1)])
        op("scalar", lambda e: e.activation(
            out=s_f2[:], in_=s_f1[:], func=AF.Sigmoid, scale=s_su[:, 1:2]),
           waits=W(vd), incs=[(sd, 1)])
        op("vector", lambda e: e.tensor_scalar(
            out=s_f2[:], in0=s_f2[:], scalar1=65536.0, scalar2=MAGIC,
            op0=ALU.mult, op1=ALU.add),
           waits=W(sd), incs=[(vd, 1)])
        op("vector", lambda e: e.tensor_scalar(
            out=s_f2[:], in0=s_f2[:], scalar1=MAGIC, scalar2=65535.0,
            op0=ALU.subtract, op1=ALU.min),
           waits=W(vd), incs=[(vd, 1)])
        op("vector", lambda e: e.tensor_scalar(
            out=s_f1[:], in0=s_f1[:], scalar1=s_su[:, 1:2], scalar2=None,
            op0=ALU.mult),
           waits=W(vd), incs=[(vd, 1)])
        op("vector", lambda e: e.scalar_tensor_tensor(
            out=s_f1[:], in0=s_f2[:], scalar=1.0 / 65536.0, in1=s_f1[:],
            op0=ALU.mult, op1=ALU.mult),
           waits=W(vd), incs=[(vd, 1)])
        op("vector", lambda e: e.reduce_max(
            out=s_c2[:], in_=s_f1[:], axis=AX.X, apply_absolute_value=True),
           waits=W(vd), incs=[(vd, 1)])
        op("vector", lambda e: e.tensor_tensor(
            out=s_mx[:, 3:4], in0=s_mx[:, 3:4], in1=s_c2[:], op=ALU.max),
           waits=W(vd), incs=[(vd, 1)])
        op("sync", lambda e, lt=lt: e.dma_start(
            out=d["t_sp"][lt * 128:(lt + 1) * 128, :], in_=s_f1[:]),
           waits=W(vd), incs=[(st, 16)])

    op("tensor", lambda e: e.transpose(p_tr2[0:1, :], s_mx[:, 3:4], s_id[:]),
       waits=W(vd), incs=[(td, 1)])
    op("vector", lambda e: e.reduce_max(out=s_red[0:1, :], in_=p_tr2[0:1, :],
                                        axis=AX.X),
       waits=W(td), incs=[(vd, 1)])
    op("sync", lambda e: e.dma_start(out=d["mx3_i"], in_=s_red[0:1, :]),
       waits=W(vd), incs=[(st, 16)])
    op("gpsimd", lambda e: e.collective_compute(
        "AllReduce", ALU.max, replica_groups=[[0, 1, 2, 3, 4, 5, 6, 7]],
        ins=[d["mx3_i"]], outs=[d["mx3_o"]]),
       waits=W(st), incs=[(gd, 1)])
    op("sync", lambda e: e.dma_start(out=s_gmax[0:1, 0:1], in_=d["mx3_o"]),
       waits=W(gd), incs=[(ld, 16)])
    op("tensor", lambda e: e.matmul(p_c1[:, 0:1], s_ones[0:1, :],
                                    s_gmax[0:1, 0:1], start=True, stop=True),
       waits=W(ld, vd), incs=[(td, 1)])
    op("vector", lambda e: e.tensor_scalar(
        out=s_su[:, 4:5], in0=p_c1[:, 0:1], scalar1=1.0 / 32767.0,
        scalar2=1e-12, op0=ALU.mult, op1=ALU.max),
       waits=W(td), incs=[(vd, 1)])
    op("vector", lambda e: e.reciprocal(out=s_su[:, 5:6], in_=s_su[:, 4:5]),
       waits=W(vd), incs=[(vd, 1)])

    if TAIL_UPTO == "p2":
        return
    op("vector", lambda e: e.memset(s_mx[:, 3:4], 0.0), incs=[(vd, 1)])
    for lt in range(NTT):
        op("sync", lambda e, lt=lt: e.dma_start(
            out=s_f1[:], in_=d["t_sp"][lt * 128:(lt + 1) * 128, :]),
           waits=W(vd), incs=[(ld, 16)])
        op("sync", lambda e, lt=lt: e.dma_start(
            out=s_f2[:], in_=d["up_sp"][lt * 128:(lt + 1) * 128, :]),
           waits=W(vd), incs=[(ld, 16)])
        op("vector", lambda e: e.tensor_scalar(
            out=s_f1[:], in0=s_f1[:], scalar1=s_su[:, 5:6], scalar2=None,
            op0=ALU.mult),
           waits=W(ld), incs=[(vd, 1)])
        op("vector", lambda e: e.tensor_scalar(
            out=s_f1[:], in0=s_f1[:], scalar1=MAGIC, scalar2=MAGIC,
            op0=ALU.add, op1=ALU.subtract),
           waits=W(vd), incs=[(vd, 1)])
        op("vector", lambda e: e.tensor_scalar(
            out=s_f2[:], in0=s_f2[:], scalar1=s_su[:, 2:3], scalar2=None,
            op0=ALU.mult),
           waits=W(vd), incs=[(vd, 1)])
        op("vector", lambda e: e.tensor_scalar(
            out=s_f2[:], in0=s_f2[:], scalar1=MAGIC, scalar2=MAGIC,
            op0=ALU.add, op1=ALU.subtract),
           waits=W(vd), incs=[(vd, 1)])
        op("vector", lambda e: e.tensor_scalar(
            out=s_f2[:], in0=s_f2[:], scalar1=s_su[:, 0:1], scalar2=None,
            op0=ALU.mult),
           waits=W(vd), incs=[(vd, 1)])
        op("vector", lambda e: e.scalar_tensor_tensor(
            out=s_f1[:], in0=s_f1[:], scalar=s_su[:, 4:5], in1=s_f2[:],
            op0=ALU.mult, op1=ALU.mult),
           waits=W(vd), incs=[(vd, 1)])
        op("vector", lambda e: e.reduce_max(
            out=s_c2[:], in_=s_f1[:], axis=AX.X, apply_absolute_value=True),
           waits=W(vd), incs=[(vd, 1)])
        op("vector", lambda e: e.tensor_tensor(
            out=s_mx[:, 3:4], in0=s_mx[:, 3:4], in1=s_c2[:], op=ALU.max),
           waits=W(vd), incs=[(vd, 1)])
    op("tensor", lambda e: e.transpose(p_tr2[0:1, :], s_mx[:, 3:4], s_id[:]),
       waits=W(vd), incs=[(td, 1)])
    op("vector", lambda e: e.reduce_max(out=s_red[0:1, :], in_=p_tr2[0:1, :],
                                        axis=AX.X),
       waits=W(td), incs=[(vd, 1)])
    op("sync", lambda e: e.dma_start(out=d["mx4_i"], in_=s_red[0:1, :]),
       waits=W(vd), incs=[(st, 16)])
    op("gpsimd", lambda e: e.collective_compute(
        "AllReduce", ALU.max, replica_groups=[[0, 1, 2, 3, 4, 5, 6, 7]],
        ins=[d["mx4_i"]], outs=[d["mx4_o"]]),
       waits=W(st), incs=[(gd, 1)])
    op("sync", lambda e: e.dma_start(out=s_gmax[0:1, 0:1], in_=d["mx4_o"]),
       waits=W(gd), incs=[(ld, 16)])
    op("tensor", lambda e: e.matmul(p_c1[:, 0:1], s_ones[0:1, :],
                                    s_gmax[0:1, 0:1], start=True, stop=True),
       waits=W(ld, vd), incs=[(td, 1)])
    op("vector", lambda e: e.tensor_scalar(
        out=s_su[:, 6:7], in0=p_c1[:, 0:1], scalar1=1.0 / 32767.0,
        scalar2=1e-12, op0=ALU.mult, op1=ALU.max),
       waits=W(td), incs=[(vd, 1)])
    op("vector", lambda e: e.reciprocal(out=s_su[:, 7:8], in_=s_su[:, 6:7]),
       waits=W(vd), incs=[(vd, 1)])

    if TAIL_UPTO == "p3":
        return
    scC = type(sc)()
    s_wd = scC.sb("wdr", (128, 24, DM), BF16)
    for k in range(24):
        op("sync", lambda e, k=k: e.dma_start(
            out=s_wd[:, k, :], in_=d["wdT_o"][k * 128:(k + 1) * 128, :]),
           waits=W(gd), incs=[(ld, 16)])
    wd_ev = (ld, ld.v)
    s_f1b = [s_f1, scC.sb("f1B", (128, FI))]
    s_f2b = [s_f2, scC.sb("f2B", (128, FI))]
    s_ob2 = [scC.sb("obA", (128, FI), BF16), scC.sb("obB", (128, FI), BF16)]
    s_oT2 = [scC.sb("oTA", (128, 24, 128), BF16),
             scC.sb("oTB", (128, 24, 128), BF16)]
    s_x1f = [s_x1, scC.sb("x1fB", (128, DM))]
    s_xtf = [scC.sb("xtfA", (128, DM)), scC.sb("xtfB", (128, DM))]
    s_stgf = [s_stg, scC.sb("stgfB", (128, DM))]
    p_of = [p_o, scC.psum("p_ofB", (128, 1024))]
    p_trf = [p_trb, scC.psum("p_trfB", (128, 128), BF16)]
    f_free = [None, None]
    ob_free = [None, None]
    oT_free = [None, None]
    po_free = [None, None]
    stgf_free = [None, None]
    x1f_free = [None, None]
    trp_free = [None, None]
    su_ev = (vd, vd.v)
    for lt in range(NTT):
        sl = lt % 2
        wld = ([f_free[sl]] if f_free[sl] else [])
        op("sync", lambda e, lt=lt, sl=sl: e.dma_start(
            out=s_f1b[sl][:], in_=d["t_sp"][lt * 128:(lt + 1) * 128, :]),
           waits=wld, incs=[(ld, 16)])
        op("sync", lambda e, lt=lt, sl=sl: e.dma_start(
            out=s_f2b[sl][:], in_=d["up_sp"][lt * 128:(lt + 1) * 128, :]),
           waits=wld, incs=[(ld, 16)])
        fld = (ld, ld.v)
        f1, f2 = s_f1b[sl], s_f2b[sl]
        op("vector", lambda e, f1=f1: e.tensor_scalar(
            out=f1[:], in0=f1[:], scalar1=s_su[:, 5:6], scalar2=None,
            op0=ALU.mult),
           waits=[fld, su_ev], incs=[(vd, 1)])
        op("vector", lambda e, f1=f1: e.tensor_scalar(
            out=f1[:], in0=f1[:], scalar1=MAGIC, scalar2=MAGIC,
            op0=ALU.add, op1=ALU.subtract),
           waits=W(vd), incs=[(vd, 1)])
        op("vector", lambda e, f2=f2: e.tensor_scalar(
            out=f2[:], in0=f2[:], scalar1=s_su[:, 2:3], scalar2=None,
            op0=ALU.mult),
           waits=W(vd), incs=[(vd, 1)])
        op("vector", lambda e, f2=f2: e.tensor_scalar(
            out=f2[:], in0=f2[:], scalar1=MAGIC, scalar2=MAGIC,
            op0=ALU.add, op1=ALU.subtract),
           waits=W(vd), incs=[(vd, 1)])
        op("vector", lambda e, f2=f2: e.tensor_scalar(
            out=f2[:], in0=f2[:], scalar1=s_su[:, 0:1], scalar2=None,
            op0=ALU.mult),
           waits=W(vd), incs=[(vd, 1)])
        op("vector", lambda e, f1=f1, f2=f2: e.scalar_tensor_tensor(
            out=f1[:], in0=f1[:], scalar=s_su[:, 4:5], in1=f2[:],
            op0=ALU.mult, op1=ALU.mult),
           waits=W(vd), incs=[(vd, 1)])
        op("vector", lambda e, f1=f1: e.tensor_scalar(
            out=f1[:], in0=f1[:], scalar1=s_su[:, 7:8], scalar2=None,
            op0=ALU.mult),
           waits=W(vd), incs=[(vd, 1)])
        op("vector", lambda e, f1=f1: e.tensor_scalar(
            out=f1[:], in0=f1[:], scalar1=MAGIC, scalar2=MAGIC,
            op0=ALU.add, op1=ALU.subtract),
           waits=W(vd), incs=[(vd, 1)])
        f_free[sl] = (vd, vd.v)
        wob = [ob_free[sl]] if ob_free[sl] else []
        op("vector", lambda e, f1=f1, sl=sl: e.tensor_scalar(
            out=s_ob2[sl][:], in0=f1[:], scalar1=s_su[:, 6:7], scalar2=None,
            op0=ALU.mult),
           waits=list(W(vd)) + wob, incs=[(vd, 1)])
        ob_ev = (vd, vd.v)
        tr_evs = []
        for j in range(24):
            jp = j % 2
            wtr = [ob_ev] + ([trp_free[jp]] if trp_free[jp] else []) \
                + ([oT_free[sl]] if (j == 0 and oT_free[sl]) else [])
            op("tensor", lambda e, j=j, sl=sl, jp=jp: e.transpose(
                p_trf[jp][:], s_ob2[sl][:, j * 128:(j + 1) * 128], s_idb[:]),
               waits=wtr, incs=[(td, 1)])
            tr_ev = (td, td.v)
            op("scalar", lambda e, j=j, sl=sl, jp=jp: e.copy(
                out=s_oT2[sl][:, j, :], in_=p_trf[jp][:]),
               waits=[tr_ev], incs=[(sd, 1)])
            trp_free[jp] = (sd, sd.v)
        ob_free[sl] = (td, td.v)
        cp_done = (sd, sd.v)
        wmm = [cp_done, wd_ev] + ([po_free[sl]] if po_free[sl] else [])
        op("tensor", lambda e, sl=sl: [
            e.matmul(p_of[sl][:, jh * 512:(jh + 1) * 512], s_oT2[sl][:, k, :],
                     s_wd[:, k, jh * 512:(jh + 1) * 512],
                     start=(k == 0), stop=(k == 23))
            for jh in range(2) for k in range(24)][-1],
           waits=wmm, incs=[(td, 1)])
        mm_ev = (td, td.v)
        oT_free[sl] = mm_ev
        wstg = [stgf_free[sl]] if stgf_free[sl] else []
        op("sync", lambda e, lt=lt, sl=sl: e.dma_start(
            out=s_stgf[sl][:], in_=d["x1_sp"][lt * 128:(lt + 1) * 128, :]),
           waits=wstg, incs=[(ld, 16)])
        stg_ev = (ld, ld.v)
        wx1 = ([x1f_free[sl]] if x1f_free[sl] else [])
        xadd, fin = layernorm_tile(p_of[sl][:], s_x1f[sl][:], None, s_c1,
                                   s_c2, s_c3, s_sq, s_stgf[sl],
                                   s_xtf[sl],
                                   [mm_ev, stg_ev] + wx1)
        po_free[sl] = xadd
        stgf_free[sl] = xadd
        op("sync", lambda e, lt=lt, sl=sl: e.dma_start(
            out=d["out"][lt * 128:(lt + 1) * 128, :], in_=s_x1f[sl][:]),
           waits=[fin], incs=[(st, 16)])
        x1f_free[sl] = (st, st.v)
    P.emit(); P.clear()
    scC.close()


def make_in_maps(inputs):
    ip = inputs
    x = np.asarray(ip["L0_attn_hidden_states"], np.float32)
    w_in = np.asarray(ip["L0_attn_in_proj_w"], np.float32)
    conv_w = np.asarray(ip["L0_attn_conv_w"], np.float32)
    conv_b = np.asarray(ip["L0_attn_conv_b"], np.float32)
    xproj = np.asarray(ip["L0_attn_x_proj_w"], np.float32)
    A_log = np.asarray(ip["L0_attn_A_log"], np.float32)
    Dv = np.asarray(ip["L0_attn_D"], np.float32)
    dtow = np.asarray(ip["L0_attn_dt_out_w"], np.float32)
    dtob = np.asarray(ip["L0_attn_dt_out_b"], np.float32)
    outw = np.asarray(ip["L0_attn_out_proj_w"], np.float32)
    wg = np.asarray(ip["L0_ffn_w_gate"], np.float32)
    wu = np.asarray(ip["L0_ffn_w_up"], np.float32)
    wd = np.asarray(ip["L0_ffn_w_down"], np.float32)

    maps = []
    for c in range(8):
        b, h = c // 2, c % 2
        sl = slice(h * DH, (h + 1) * DH)
        m = {}
        m["xT"] = np.ascontiguousarray(x[b].T)
        m["x_nat"] = np.ascontiguousarray(x[b, h * LT:(h + 1) * LT])
        m["w_in_T"] = np.ascontiguousarray(
            np.concatenate([w_in[sl].T, w_in[DI + h * DH:DI + (h + 1) * DH].T], axis=1))
        m["conv_w"] = np.ascontiguousarray(
            conv_w[sl].reshape(NT, 128, 4).transpose(1, 0, 2).reshape(128, NT * 4))
        m["conv_b"] = np.ascontiguousarray(conv_b[sl].reshape(NT, 128).T)
        m["xproj_T"] = np.ascontiguousarray(xproj[:, sl].T)
        m["A_logc"] = np.ascontiguousarray(
            A_log[sl].reshape(NT, 128, N).transpose(1, 0, 2).reshape(128, NT * N))
        m["Dc"] = np.ascontiguousarray(Dv[sl].reshape(NT, 128).T)
        m["dtbc"] = np.ascontiguousarray(dtob[sl].reshape(NT, 128).T)
        m["dtoutw_T"] = np.ascontiguousarray(dtow[sl].T)
        m["outw_T"] = np.ascontiguousarray(outw.T)
        m["m_in_wT"] = np.ascontiguousarray(np.asarray(ip["L0_attn_m_in_w"], np.float32).T)
        m["m_conv_w"] = np.ascontiguousarray(np.asarray(ip["L0_attn_m_conv_w"], np.float32))
        m["m_conv_b"] = np.ascontiguousarray(
            np.asarray(ip["L0_attn_m_conv_b"], np.float32)[:, None])
        m["m_xp_T"] = np.ascontiguousarray(np.asarray(ip["L0_attn_m_x_proj_w"], np.float32).T)
        m["m_dt_wT"] = np.ascontiguousarray(np.asarray(ip["L0_attn_m_dt_w"], np.float32).T)
        m["m_dt_b"] = np.ascontiguousarray(
            np.asarray(ip["L0_attn_m_dt_b"], np.float32)[:, None])
        m["mA"] = np.ascontiguousarray(np.asarray(ip["L0_attn_m_A_log"], np.float32))
        m["m_D"] = np.ascontiguousarray(np.asarray(ip["L0_attn_m_D"], np.float32)[:, None])
        m["m_outwT"] = np.ascontiguousarray(np.asarray(ip["L0_attn_m_out_w"], np.float32).T)
        m["hoff"] = np.array([[h]], np.int32)
        m["wg_s"] = np.ascontiguousarray(wg[:, c * 128:(c + 1) * 128])
        m["wu_s"] = np.ascontiguousarray(wu[:, c * 128:(c + 1) * 128])
        m["wd_s"] = np.ascontiguousarray(wd[:, c * (FI // 8):(c + 1) * (FI // 8)])
        maps.append(m)
    return maps


def kernel(**inputs):
    maps = make_in_maps(inputs)
    nc = build_nc()
    res = run_bass_kernel_spmd(nc, maps, core_ids=list(range(8)))
    out = np.empty((B, L, DM), np.float32)
    for c in range(8):
        b, h = c // 2, c % 2
        out[b, h * LT:(h + 1) * LT] = res.results[c]["out"]
    return out


def kernel_debug(**inputs):
    maps = make_in_maps(inputs)
    nc = build_nc()
    res = run_bass_kernel_spmd(nc, maps, core_ids=list(range(8)))
    return res.results
